# version 05 (final)
# baseline (speedup 1.0000x reference)
import numpy as np
import ml_dtypes
from contextlib import ExitStack

import concourse.bass as bass
import concourse.mybir as mybir
from concourse.bass_utils import run_bass_kernel_spmd
from concourse.bass import ds
from concourse.expressions import RuntimeValue

F32 = mybir.dt.float32
BF16 = mybir.dt.bfloat16
I32 = mybir.dt.int32
AF = mybir.ActivationFunctionType
ALU = mybir.AluOpType
AX = mybir.AxisListType

B, S, D = 2, 4096, 4096
H, HD = 32, 128
TPC = 1024
CLIP = 8.0
EPS = 1e-5
SCALE = float(np.float32(HD ** -0.5))
LN_SCALE = float(np.log(np.float64(HD) ** -0.5))
SLOPES = [2.0 ** (-(i + 1) / 4.0) for i in range(H)]
RELW = 4480
MASKNEG = -1e9

DEBUG_DUMP = False
DYN_C = True
C_ITERS = 99
C_TAIL = True
STOP_AFTER = 9
ENGS = ("T", "V", "S", "P", "Q")
OWNSEM = {"T": "sT", "V": "sV", "S": "sS", "P": "sP", "Q": "sQ"}
ENG_ATTR = {"T": "tensor", "V": "vector", "S": "scalar", "P": "gpsimd", "Q": "sync"}
GROUPS = [[0, 1, 2, 3], [4, 5, 6, 7]]


class Prog:

    def __init__(self, nc, es):
        self.nc = nc
        self.es = es
        self.blocks = [{e: [] for e in ENGS}]
        self.cnt = {}
        self.wm = {e: {} for e in ENGS}
        self.sems = {}
        self.shared = {}

    def _sem(self, name):
        if name not in self.sems:
            self.sems[name] = self.es.enter_context(self.nc.semaphore(name))
        return self.sems[name]

    def begin_block(self):
        snap = [(s, v) for s, v in self.cnt.items()]
        self.blocks.append({e: [] for e in ENGS})
        for e in ENGS:
            self.wait_only(e, snap)

    def _waits(self, eng, deps):
        agg = {}
        for d in deps:
            if d is None:
                continue
            s, v = d
            if v > agg.get(s, 0):
                agg[s] = v
        waits = []
        for s, v in agg.items():
            if self.wm[eng].get(s, 0) >= v:
                continue
            waits.append((s, v))
            self.wm[eng][s] = v
        return waits

    def op(self, eng, fn, deps=(), sem=None, inc=1):
        sem = sem or OWNSEM[eng]
        self._sem(sem)
        waits = self._waits(eng, deps)
        self.cnt[sem] = self.cnt.get(sem, 0) + inc
        tok = (sem, self.cnt[sem])
        self.blocks[-1][eng].append((waits, fn, sem, inc))
        return tok

    def wait_only(self, eng, deps):
        waits = self._waits(eng, deps)
        if waits:
            self.blocks[-1][eng].append((waits, None, None, 0))

    def dma(self, eng, out, in_, deps=(), **kw):
        return self.op(eng, lambda e: e.dma_start(out=out, in_=in_, **kw),
                       deps, sem="d" + eng, inc=16)

    def dma_dyn(self, eng, fn, deps=()):
        return self.op(eng, fn, deps, sem="d" + eng, inc=16)

    def replay(self):
        for blk in self.blocks:
            if not any(blk[e] for e in ENGS):
                continue
            with self.nc.Block() as block:
                for ek in ENGS:
                    ops = blk[ek]
                    if not ops:
                        continue

                    def mk(ops=ops):
                        def body(eng):
                            for waits, fn, sem, inc in ops:
                                for (s, v) in waits:
                                    eng.wait_ge(self.sems[s], v)
                                if fn is not None:
                                    inst = fn(eng)
                                    if inst is not None:
                                        inst.then_inc(self.sems[sem], inc)
                        return body

                    getattr(block, ENG_ATTR[ek])(mk())


def build_program():
    nc = bass.Bass("TRN2", target_bir_lowering=False, debug=False, num_devices=8, detect_race_conditions=False)
    es = ExitStack()
    es.__enter__()
    p = Prog(nc, es)

    def mk_alloc(stack):
        return (lambda name, shape, dt: stack.enter_context(nc.sbuf_tensor(name, shape, dt)),
                lambda name, shape, dt=F32: stack.enter_context(nc.psum_tensor(name, shape, dt)))

    d_x = nc.dram_tensor("x_rows", [TPC, D], F32, kind="ExternalInput").ap()
    d_wqkv = nc.dram_tensor("wqkv_g", [48, 32, 128, 256], F32, kind="ExternalInput").ap()
    d_wbase = nc.dram_tensor("wbase_g", [16, 32, 128, 256], F32, kind="ExternalInput").ap()
    d_wout = nc.dram_tensor("wout_g", [16, 32, 128, 256], F32, kind="ExternalInput").ap()
    d_wsm = nc.dram_tensor("wsm_g", [32, 128, 137], F32, kind="ExternalInput").ap()
    d_bcat = nc.dram_tensor("bcat", [128, 4096], F32, kind="ExternalInput").ap()
    d_bbase = nc.dram_tensor("b_base_pc", [128, 32], F32, kind="ExternalInput").ap()
    d_qlw = nc.dram_tensor("q_ln_w_pc", [128, 32], F32, kind="ExternalInput").ap()
    d_qlb = nc.dram_tensor("q_ln_b_pc", [128, 32], F32, kind="ExternalInput").ap()
    d_klw = nc.dram_tensor("k_ln_w_pc", [128, 32], F32, kind="ExternalInput").ap()
    d_klb = nc.dram_tensor("k_ln_b_pc", [128, 32], F32, kind="ExternalInput").ap()
    d_bthr = nc.dram_tensor("b_thr_in", [1, 1], F32, kind="ExternalInput").ap()
    d_relm = nc.dram_tensor("relmaster", [128, RELW], F32, kind="ExternalInput").ap()
    d_idb = nc.dram_tensor("ident_bf", [128, 128], BF16, kind="ExternalInput").ap()
    d_idf = nc.dram_tensor("ident_f32", [128, 128], F32, kind="ExternalInput").ap()
    d_slope = nc.dram_tensor("slopes_bc", [128, 8], F32, kind="ExternalInput").ap()
    d_selq = nc.dram_tensor("selq", [1, 512], F32, kind="ExternalInput").ap()
    d_selk = nc.dram_tensor("selk", [1, 512], F32, kind="ExternalInput").ap()
    d_rank = nc.dram_tensor("rank_in", [1, 1], I32, kind="ExternalInput").ap()
    d_doffs = nc.dram_tensor("dyn_offs", [1, 32], I32, kind="ExternalInput").ap()
    d_out = nc.dram_tensor("out_rows", [TPC, D], F32, kind="ExternalOutput").ap()

    d_qkgin = nc.dram_tensor("qk_gin", [2 * D, TPC], BF16, kind="Internal").ap()
    d_vgin = nc.dram_tensor("v_gin", [D, TPC], BF16, kind="Internal").ap()
    d_qkgout = nc.dram_tensor("qk_gout", [4 * 2 * D, TPC], BF16, kind="Internal").ap()
    d_vgout = nc.dram_tensor("v_gout", [4 * D, TPC], BF16, kind="Internal").ap()
    d_nin = nc.dram_tensor("n_in", [1, 64], F32, kind="Internal").ap()
    d_nout = nc.dram_tensor("n_out", [1, 64], F32, kind="Internal").ap()
    d_gscr = nc.dram_tensor("g_scr", [D, TPC], F32, kind="Internal").ap()
    d_ogin = nc.dram_tensor("o_gin", [8 * HD, S], BF16, kind="Internal").ap()
    d_qmine = nc.dram_tensor("q_mine", [4 * 1024, TPC], BF16, kind="Internal").ap()
    d_kmine = nc.dram_tensor("k_mine", [4 * 1024, TPC], BF16, kind="Internal").ap()
    d_vmine = nc.dram_tensor("v_mine", [4 * 1024, TPC], BF16, kind="Internal").ap()
    d_ogout = nc.dram_tensor("o_gout", [4 * 8 * HD, S], BF16, kind="Internal").ap()

    dbg_names = []
    if DEBUG_DUMP:
        dbg_qk = nc.dram_tensor("dbg_qk", [2 * D, TPC], BF16, kind="ExternalOutput").ap()
        dbg_v = nc.dram_tensor("dbg_v", [D, TPC], BF16, kind="ExternalOutput").ap()
        dbg_g = nc.dram_tensor("dbg_g", [D, TPC], F32, kind="ExternalOutput").ap()
        dbg_n = nc.dram_tensor("dbg_n", [1, 64], F32, kind="ExternalOutput").ap()
        dbg_o = nc.dram_tensor("dbg_o", [8 * HD, S], BF16, kind="ExternalOutput").ap()
        dbg_names = ["dbg_qk", "dbg_v", "dbg_g", "dbg_n", "dbg_o"]

    sbuf, _ = mk_alloc(es)
    c_idb = sbuf("c_idb", [128, 128], BF16)
    c_idf = sbuf("c_idf", [128, 128], F32)
    c_bbase = sbuf("c_bbase", [128, 32], F32)
    c_qlw = sbuf("c_qlw", [128, 32], F32)
    c_qlb = sbuf("c_qlb", [128, 32], F32)
    c_klw = sbuf("c_klw", [128, 32], F32)
    c_klb = sbuf("c_klb", [128, 32], F32)
    c_onesb = sbuf("c_onesb", [128, 1], BF16)
    c_onesfr = sbuf("c_onesfr", [1, 128], F32)
    c_bthrn = sbuf("c_bthrn", [128, 1], F32)
    c_slope = sbuf("c_slope", [128, 8], F32)
    c_selq = sbuf("c_selq", [1, 512], F32)
    c_selk = sbuf("c_selk", [1, 512], F32)
    c_rank = sbuf("c_rank", [1, 1], I32)
    c_offs = sbuf("c_offs", [1, 32], I32)
    s_norm = sbuf("s_norm", [1, 64], F32)
    s_negC = sbuf("s_negC", [128, 8], F32)
    s_row = sbuf("s_row", [1, 512], F32)
    s_r64a = sbuf("s_r64a", [1, 64], F32)
    s_r64b = sbuf("s_r64b", [1, 64], F32)
    s_q8 = sbuf("s_q8", [1, 8], F32)
    s_k8 = sbuf("s_k8", [1, 8], F32)
    s_mu_b = sbuf("s_mu_b", [1, 512], F32)
    s_rs_b = sbuf("s_rs_b", [1, 512], F32)
    c_eps = sbuf("c_eps", [128, 1], F32)

    tk_id = p.dma("Q", c_idb[:, :], d_idb)
    tk_idf = p.dma("Q", c_idf[:, :], d_idf)
    tk_bbase = p.dma("Q", c_bbase[:, :], d_bbase)
    tk_qlw = p.dma("Q", c_qlw[:, :], d_qlw)
    tk_qlb = p.dma("Q", c_qlb[:, :], d_qlb)
    tk_klw = p.dma("Q", c_klw[:, :], d_klw)
    tk_klb = p.dma("Q", c_klb[:, :], d_klb)
    tk_slope = p.dma("Q", c_slope[:, :], d_slope)
    tk_selq = p.dma("Q", c_selq[:, :], d_selq)
    tk_selk = p.dma("Q", c_selk[:, :], d_selk)
    tk_rk = p.dma("Q", c_rank[:, :], d_rank)
    tk_offs = p.dma("Q", c_offs[:, :], d_doffs)
    tk_btr = p.dma("Q", s_row[0:1, 0:1], d_bthr)
    tk_o1 = p.op("P", lambda e: e.memset(c_onesb[:, :], 1.0))
    tk_o2 = p.op("P", lambda e: e.memset(c_onesfr[:, :], 1.0))
    tk_eps = p.op("P", lambda e: e.memset(c_eps[:, :], EPS))

    es_ab = ExitStack()
    sbuf_ab, _ = mk_alloc(es_ab)
    xnT = sbuf_ab("xnT", [128, 32 * TPC], BF16)
    hwT = sbuf_ab("hwT", [128, TPC], BF16)

    es_b0 = ExitStack()
    sbuf0, psum0 = mk_alloc(es_b0)
    a_x = sbuf0("a_x", [128, D], F32)
    a_scr = sbuf0("a_scr", [128, D], F32)
    a_st = sbuf0("a_st", [128, 12], F32)
    a_wsm = sbuf0("a_wsm", [128, 32 * 137], BF16)
    a_g8 = sbuf0("a_g8", [128, 8], F32)
    a_gst = sbuf0("a_gst", [128, 8], F32)
    a_hw = sbuf0("a_hw", [128, 128], BF16)
    ps_tr0 = psum0("ps_tr0", [128, 128], F32)
    ps_tr1 = psum0("ps_tr1", [128, 128], F32)
    ps_trb = psum0("ps_trb", [128, 128], BF16)
    ps_sm = psum0("ps_sm", [128, 137], F32)
    ps_b1 = psum0("ps_b1", [128, 8], F32)

    for q in range(4):
        tk_wsm = p.dma("P", a_wsm[:, :].rearrange("p (c f) -> p c f", f=137)[:, q * 8:(q + 1) * 8, :],
                       d_wsm.rearrange("c p f -> p c f")[:, q * 8:(q + 1) * 8, :])
    tnb = p.op("V", lambda e: e.tensor_scalar_mul(s_row[0:1, 1:2], s_row[0:1, 0:1], -1.0), deps=[tk_btr])
    tbm = p.op("T", lambda e: e.matmul(ps_b1[:, 0:1], c_onesfr[:, :], s_row[0:1, 1:2],
                                       start=True, stop=True), deps=[tk_o2, tnb])
    tk_bthr = p.op("S", lambda e: e.copy(c_bthrn[:, :], ps_b1[:, 0:1]), deps=[tbm])

    ev_prev = [None, None]
    hw_prev = None
    smread_prev = []
    prev_x_users = []
    hwT_done = []
    for tt in range(8):
        xl = p.dma("Q", a_x[:, :], d_x[tt * 128:(tt + 1) * 128, :], deps=prev_x_users)
        sq = p.op("S", lambda e: e.activation(a_scr[:, :], a_x[:, :], AF.Square), deps=[xl])
        sm = p.op("V", lambda e: e.reduce_sum(a_st[:, 0:1], a_x[:, :], axis=AX.X), deps=[xl])
        s2 = p.op("V", lambda e: e.reduce_sum(a_st[:, 1:2], a_scr[:, :], axis=AX.X), deps=[sq, sm])
        mu = p.op("V", lambda e: e.tensor_scalar_mul(a_st[:, 2:3], a_st[:, 0:1], 1.0 / D), deps=[s2])
        m2 = p.op("V", lambda e: e.tensor_tensor(a_st[:, 3:4], a_st[:, 2:3], a_st[:, 2:3], ALU.mult), deps=[mu])
        va = p.op("V", lambda e: e.tensor_scalar(a_st[:, 4:5], a_st[:, 1:2], 1.0 / D, None, ALU.mult), deps=[m2])
        vb = p.op("V", lambda e: e.tensor_tensor(a_st[:, 5:6], a_st[:, 4:5], a_st[:, 3:4], ALU.subtract), deps=[va])
        lv = p.op("S", lambda e: e.activation(a_st[:, 6:7], a_st[:, 5:6], AF.Ln, bias=c_eps[:, 0:1]), deps=[vb, tk_eps])
        rs = p.op("S", lambda e: e.activation(a_st[:, 7:8], a_st[:, 6:7], AF.Exp, scale=-0.5), deps=[lv])
        xn = p.op("V", lambda e: e.tensor_scalar(a_scr[:, :], a_x[:, :], a_st[:, 2:3], a_st[:, 7:8],
                                                 ALU.subtract, ALU.mult), deps=[rs, vb])
        prev_x_users = [xn, sq, sm]
        for dc in range(32):
            sel = dc % 2
            pst = ps_tr0 if sel == 0 else ps_tr1
            tr = p.op("T", lambda e, dc=dc, pst=pst: e.transpose(
                pst[:, :], a_scr[:, dc * 128:(dc + 1) * 128], c_idf[:, :]),
                deps=[xn, tk_idf, ev_prev[sel]])
            col = dc * 1024 + tt * 128
            if sel == 0:
                ev = p.op("V", lambda e, col=col, pst=pst: e.tensor_copy(
                    xnT[:, col:col + 128], pst[:, :]), deps=[tr])
            else:
                ev = p.op("S", lambda e, col=col, pst=pst: e.copy(
                    xnT[:, col:col + 128], pst[:, :]), deps=[tr])
            ev_prev[sel] = ev
        xnT_done = [ev_prev[0], ev_prev[1]]

        mm_sm = None
        for dc in range(32):
            dep0 = ([xnT_done[0], xnT_done[1], tk_wsm] + smread_prev) if dc == 0 else []
            mm_sm = p.op("T", lambda e, dc=dc, tt=tt: e.matmul(
                ps_sm[:, :], xnT[:, dc * 1024 + tt * 128: dc * 1024 + tt * 128 + 128],
                a_wsm[:, dc * 137:(dc + 1) * 137], start=(dc == 0), stop=(dc == 31)), deps=dep0)
        gmx = p.op("V", lambda e: e.reduce_max(a_gst[:, 0:1], ps_sm[:, 0:8], axis=AX.X), deps=[mm_sm])
        gmn = p.op("V", lambda e: e.tensor_scalar_mul(a_gst[:, 1:2], a_gst[:, 0:1], -1.0), deps=[gmx])
        gex = p.op("S", lambda e: e.activation(a_g8[:, :], ps_sm[:, 0:8], AF.Exp, bias=a_gst[:, 1:2],
                                               accum_out=a_gst[:, 2:3]), deps=[gmn])
        grc = p.op("V", lambda e: e.reciprocal(a_gst[:, 3:4], a_gst[:, 2:3]), deps=[gex])
        gnm = p.op("V", lambda e: e.tensor_scalar_mul(a_g8[:, :], a_g8[:, :], a_gst[:, 3:4]), deps=[grc])
        te = p.op("S", lambda e: e.activation(a_gst[:, 4:5], ps_sm[:, 8:9], AF.Exp, bias=c_bthrn[:, 0:1],
                                              scale=-1.0), deps=[mm_sm, tk_bthr])
        ta = p.op("V", lambda e: e.tensor_scalar(a_gst[:, 5:6], a_gst[:, 4:5], 1.0, None, ALU.add), deps=[te])
        trc = p.op("V", lambda e: e.reciprocal(a_gst[:, 6:7], a_gst[:, 5:6]), deps=[ta])
        thr = p.op("V", lambda e: e.tensor_scalar_mul(a_gst[:, 7:8], a_gst[:, 6:7], 0.125), deps=[trc])
        ad = p.op("V", lambda e: e.tensor_scalar(a_g8[:, :], a_g8[:, :], a_gst[:, 7:8], 0.0,
                                                 ALU.subtract, ALU.max), deps=[thr, gnm])
        ws = p.op("V", lambda e: e.reduce_sum(a_gst[:, 0:1], a_g8[:, :], axis=AX.X), deps=[ad])
        iz = p.op("V", lambda e: e.tensor_single_scalar(a_gst[:, 1:2], a_gst[:, 0:1], 0.0, ALU.is_equal), deps=[ws])
        wz = p.op("V", lambda e: e.tensor_tensor(a_gst[:, 2:3], a_gst[:, 0:1], a_gst[:, 1:2], ALU.add), deps=[iz])
        wr = p.op("V", lambda e: e.reciprocal(a_gst[:, 3:4], a_gst[:, 2:3]), deps=[wz])
        wf = p.op("V", lambda e: e.tensor_scalar(a_g8[:, :], a_g8[:, :], a_gst[:, 3:4], 2.0,
                                                 ALU.mult, ALU.mult), deps=[wr])
        hw_toks = []
        for ee in range(8):
            hw_toks.append(p.op("V", lambda e, ee=ee: e.tensor_scalar_mul(
                a_hw[:, ee * 16:(ee + 1) * 16], ps_sm[:, 9 + ee * 16: 9 + (ee + 1) * 16],
                a_g8[:, ee:ee + 1]), deps=[wf]))
        trh = p.op("T", lambda e: e.transpose(ps_trb[:, :], a_hw[:, :], c_idb[:, :]),
                   deps=hw_toks + [tk_id, hw_prev])
        hwev = p.op("V", lambda e, tt=tt: e.tensor_copy(hwT[:, tt * 128:(tt + 1) * 128],
                                                        ps_trb[:, :]), deps=[trh])
        hw_prev = hwev
        hwT_done.append(hwev)
        smread_prev = [gmx, gex, te] + hw_toks

    if STOP_AFTER < 1:
        p.replay()
        es_b0.close()
        es_ab.close()
        es.__exit__(None, None, None)
        return nc, dbg_names
    es_b0.close()
    p.begin_block()
    es_b1 = ExitStack()
    sbuf1, psum1 = mk_alloc(es_b1)
    a_part = sbuf1("a_part", [128, 32 * TPC], BF16)
    a_w = [sbuf1("a_w0", [128, 32 * 256], BF16), sbuf1("a_w1", [128, 32 * 256], BF16)]
    a_qsq = [sbuf1("a_qsq0", [128, 512], BF16), sbuf1("a_qsq1", [128, 512], BF16)]
    a_bcm = sbuf1("a_bcm", [128, 512], F32)
    a_bcr = sbuf1("a_bcr", [128, 512], F32)
    a_tmp = [sbuf1("a_tmp0", [128, 512], F32), sbuf1("a_tmp1", [128, 512], F32)]
    a_bcat = sbuf1("a_bcat", [128, 4096], BF16)
    a_gout = [sbuf1("a_gout0", [128, 512], F32), sbuf1("a_gout1", [128, 512], F32)]
    ps_mm = [psum1(f"ps_mm{i}", [128, 512], F32) for i in range(4)]
    ps_st = [psum1(f"ps_st{i}", [1, 512], F32) for i in range(4)]

    tk_bcat = p.dma("P", a_bcat[:, :], d_bcat)

    wslab_users = [[], []]
    psmm_users = [[], [], [], []]
    psst_users = [[], [], [], []]
    a_part_writers = []
    part_write_toks = {0: [], 1: [], 2: []}
    qn_toks = []
    qsq_mm = [None, None]
    tmp_users = [None, None]
    gout_users = [[], []]

    def weight_slab(dram, sg, buf_i):
        deps = wslab_users[buf_i]
        for q in range(4):
            tok = p.dma("P", a_w[buf_i][:, :].rearrange("p (c j) -> p c j", j=256)[:, q * 8:(q + 1) * 8, :],
                        dram[sg].rearrange("c p j -> p c j")[:, q * 8:(q + 1) * 8, :],
                        deps=deps)
            deps = []
        wslab_users[buf_i] = []
        return tok

    def qkv_part(part):
        nonlocal a_part_writers
        evac_toks = []
        apply_toks = []
        stat_last = [None, None, None, None]
        mm_last = {}
        for sgl in range(16):
            sg = part * 16 + sgl
            wl = weight_slab(d_wqkv, sg, sg % 2)
            for dc in range(32):
                for ou in range(2):
                    for grp in range(2):
                        bk = ou * 2 + grp
                        dep = ([wl] + psmm_users[bk]) if dc == 0 else [wl]
                        tok = p.op("T", lambda e, dc=dc, ou=ou, grp=grp, bk=bk, sgl=sgl: e.matmul(
                            ps_mm[bk][:, :],
                            a_w[sgl % 2][:, dc * 256 + ou * 128: dc * 256 + ou * 128 + 128],
                            xnT[:, dc * 1024 + grp * 512: dc * 1024 + grp * 512 + 512],
                            start=(dc == 0), stop=(dc == 31)), deps=dep)
                        if dc == 0:
                            psmm_users[bk] = []
                        wslab_users[sg % 2].append(tok)
                        mm_last[bk] = tok
            for ou in range(2):
                osub = sgl * 2 + ou
                for grp in range(2):
                    bk = ou * 2 + grp
                    col = osub * 1024 + grp * 512
                    ev = p.op("V", lambda e, bk=bk, col=col: e.tensor_scalar(
                        a_part[:, col:col + 512], ps_mm[bk][:, :], CLIP, -CLIP,
                        ALU.min, ALU.max), deps=[mm_last[bk]] + a_part_writers)
                    psmm_users[bk].append(ev)
                    evac_toks.append((osub, grp, ev))
                    if part < 2:
                        rot = (osub * 2 + grp) % 2
                        sq_ = p.op("S", lambda e, col=col, rot=rot: e.activation(
                            a_qsq[rot][:, :], a_part[:, col:col + 512], AF.Square),
                            deps=[ev, qsq_mm[rot]])
                        t1 = p.op("T", lambda e, col=col, grp=grp, osub=osub: e.matmul(
                            ps_st[grp][:, :], c_onesb[:, :], a_part[:, col:col + 512],
                            start=(osub == 0), stop=(osub == 31)),
                            deps=[ev, tk_o1] + (psst_users[grp] if osub == 0 else []))
                        t2 = p.op("T", lambda e, grp=grp, rot=rot, osub=osub: e.matmul(
                            ps_st[2 + grp][:, :], c_onesb[:, :], a_qsq[rot][:, :],
                            start=(osub == 0), stop=(osub == 31)),
                            deps=[sq_] + (psst_users[2 + grp] if osub == 0 else []))
                        if osub == 0:
                            psst_users[grp] = []
                            psst_users[2 + grp] = []
                        qsq_mm[rot] = t2
                        stat_last[grp] = t1
                        stat_last[2 + grp] = t2
        a_part_writers = []
        if part == 2:
            for grp in range(2):
                deps = [ev for (_, g, ev) in evac_toks if g == grp]
                wtok = p.dma("Q", d_vgin.rearrange("(c p) t -> p c t", p=128)[:, :, grp * 512:(grp + 1) * 512],
                             a_part[:, :].rearrange("p (c g t) -> p c g t", g=2, t=512)[:, :, grp, :],
                             deps=deps)
                part_write_toks[part].append(wtok)
            a_part_writers = list(part_write_toks[part])
            return

        wln = c_qlw if part == 0 else c_klw
        bln = c_qlb if part == 0 else c_klb
        wln_tok = tk_qlw if part == 0 else tk_klw
        bln_tok = tk_qlb if part == 0 else tk_klb
        for grp in range(2):
            mu1 = p.op("V", lambda e, grp=grp: e.tensor_scalar(
                s_mu_b[0:1, :], ps_st[grp][0:1, :], 1.0 / D, None, ALU.mult),
                deps=[stat_last[grp]])
            va1 = p.op("V", lambda e, grp=grp: e.tensor_scalar(
                s_row[0:1, :], ps_st[2 + grp][0:1, :], 1.0 / D, None, ALU.mult),
                deps=[stat_last[2 + grp], mu1])
            psst_users[grp].append(mu1)
            psst_users[2 + grp].append(va1)
            m21 = p.op("V", lambda e: e.tensor_tensor(s_rs_b[0:1, :], s_mu_b[0:1, :],
                                                      s_mu_b[0:1, :], ALU.mult), deps=[va1])
            vd = p.op("V", lambda e: e.tensor_tensor(s_row[0:1, :], s_row[0:1, :],
                                                     s_rs_b[0:1, :], ALU.subtract), deps=[m21])
            l1 = p.op("S", lambda e: e.activation(s_row[0:1, :], s_row[0:1, :], AF.Ln, bias=c_eps[0:1, 0:1]), deps=[vd, tk_eps])
            r1 = p.op("S", lambda e: e.activation(s_rs_b[0:1, :], s_row[0:1, :], AF.Exp, scale=-0.5), deps=[l1])
            bm1 = p.op("T", lambda e: e.matmul(ps_mm[0][:, :], c_onesfr[:, :], s_mu_b[0:1, :],
                                               start=True, stop=True),
                       deps=[mu1, tk_o2] + psmm_users[0])
            bm2 = p.op("T", lambda e: e.matmul(ps_mm[1][:, :], c_onesfr[:, :], s_rs_b[0:1, :],
                                               start=True, stop=True),
                       deps=[r1] + psmm_users[1])
            psmm_users[0] = []
            psmm_users[1] = []
            cb1 = p.op("V", lambda e: e.tensor_copy(a_bcm[:, :], ps_mm[0][:, :]), deps=[bm1])
            cb2 = p.op("S", lambda e: e.copy(a_bcr[:, :], ps_mm[1][:, :]), deps=[bm2])
            psmm_users[0].append(cb1)
            psmm_users[1].append(cb2)
            for osub in range(32):
                col = osub * 1024 + grp * 512
                rot = osub % 2
                cvt = p.op("S", lambda e, col=col, rot=rot: e.copy(a_tmp[rot][:, :], a_part[:, col:col + 512]),
                           deps=[tmp_users[rot]])
                t1a = p.op("V", lambda e, rot=rot: e.tensor_tensor(a_tmp[rot][:, :], a_tmp[rot][:, :],
                                                                   a_bcm[:, :], ALU.subtract), deps=[cvt, cb1])
                t2a = p.op("V", lambda e, rot=rot: e.tensor_tensor(a_tmp[rot][:, :], a_tmp[rot][:, :],
                                                                   a_bcr[:, :], ALU.mult), deps=[t1a, cb2])
                t3a = p.op("V", lambda e, col=col, rot=rot, osub=osub: e.tensor_scalar(
                    a_part[:, col:col + 512], a_tmp[rot][:, :],
                    wln[:, osub:osub + 1], bln[:, osub:osub + 1], ALU.mult, ALU.add),
                    deps=[t2a, wln_tok, bln_tok])
                tmp_users[rot] = t3a
                sq2 = p.op("S", lambda e, col=col, rot=rot: e.activation(
                    a_qsq[rot][:, :], a_part[:, col:col + 512], AF.Square),
                    deps=[t3a, qsq_mm[rot]])
                nm = p.op("T", lambda e, rot=rot: e.matmul(ps_st[0][:, :], c_onesb[:, :],
                                                           a_qsq[rot][:, :], start=True, stop=True),
                          deps=[sq2] + psst_users[0])
                psst_users[0] = []
                qsq_mm[rot] = nm
                slot = part * 32 + osub
                if grp == 0:
                    nr = p.op("V", lambda e, slot=slot: e.reduce_max(
                        s_norm[0:1, slot:slot + 1], ps_st[0][0:1, :], axis=AX.X), deps=[nm])
                else:
                    nr0 = p.op("V", lambda e: e.reduce_max(s_row[0:1, 0:1], ps_st[0][0:1, :], axis=AX.X), deps=[nm])
                    nr = p.op("V", lambda e, slot=slot: e.tensor_tensor(
                        s_norm[0:1, slot:slot + 1], s_norm[0:1, slot:slot + 1], s_row[0:1, 0:1],
                        ALU.max), deps=[nr0])
                psst_users[0].append(nr)
                qn_toks.append(nr)
                apply_toks.append((osub, grp, t3a))
        for grp in range(2):
            deps = [t for (_, g, t) in apply_toks if g == grp]
            wtok = p.dma("Q",
                         d_qkgin.rearrange("(u c p) t -> u p c t", u=2, p=128)[part, :, :, grp * 512:(grp + 1) * 512],
                         a_part[:, :].rearrange("p (c g t) -> p c g t", g=2, t=512)[:, :, grp, :],
                         deps=deps)
            part_write_toks[part].append(wtok)
        a_part_writers = list(part_write_toks[part])

    qkv_part(0)
    qkv_part(1)
    qkv_part(2)
    ng = p.op("V", lambda e: e.tensor_scalar_max(s_norm[0:1, :], s_norm[0:1, :], 1e-12), deps=qn_toks)
    ndma = p.dma("Q", d_nin, s_norm[0:1, :], deps=[ng])

    g_write_toks = []
    for sgl in range(16):
        wl = weight_slab(d_wbase, sgl, sgl % 2)
        mm_last = {}
        for dc in range(32):
            for ou in range(2):
                for grp in range(2):
                    bk = ou * 2 + grp
                    dep = ([wl] + psmm_users[bk]) if dc == 0 else [wl]
                    tok = p.op("T", lambda e, dc=dc, ou=ou, grp=grp, bk=bk, sgl=sgl: e.matmul(
                        ps_mm[bk][:, :],
                        a_w[sgl % 2][:, dc * 256 + ou * 128: dc * 256 + ou * 128 + 128],
                        xnT[:, dc * 1024 + grp * 512: dc * 1024 + grp * 512 + 512],
                        start=(dc == 0), stop=False), deps=dep)
                    if dc == 0:
                        psmm_users[bk] = []
                    wslab_users[sgl % 2].append(tok)
        for ou in range(2):
            osub = sgl * 2 + ou
            for grp in range(2):
                bk = ou * 2 + grp
                mo = p.op("T", lambda e, osub=osub, grp=grp, bk=bk: e.matmul(
                    ps_mm[bk][:, :], a_bcat[:, osub * 128:(osub + 1) * 128],
                    hwT[:, grp * 512:(grp + 1) * 512], start=False, stop=True),
                    deps=[tk_bcat] + hwT_done)
                rot = (osub * 2 + grp) % 2
                sl = p.op("S", lambda e, bk=bk, osub=osub, rot=rot: e.activation(
                    a_gout[rot][:, :], ps_mm[bk][:, :], AF.Silu, bias=c_bbase[:, osub:osub + 1]),
                    deps=[mo, tk_bbase] + gout_users[rot])
                psmm_users[bk].append(sl)
                gw = p.dma("Q", d_gscr.rearrange("(c p) t -> p c t", p=128)[:, osub, grp * 512:(grp + 1) * 512],
                           a_gout[rot][:, :], deps=[sl])
                gout_users[rot] = [gw]
                g_write_toks.append(gw)

    cc_qk = p.op("P", lambda e: e.collective_compute(
        "AllGather", ALU.bypass, replica_groups=GROUPS,
        ins=[d_qkgin.opt()], outs=[d_qkgout.opt()]),
        deps=part_write_toks[0] + part_write_toks[1] + part_write_toks[2] + g_write_toks + [ndma],
        sem="cc", inc=1)
    cc_v = p.op("P", lambda e: e.collective_compute(
        "AllGather", ALU.bypass, replica_groups=GROUPS,
        ins=[d_vgin.opt()], outs=[d_vgout.opt()]),
        deps=part_write_toks[2], sem="cc", inc=1)
    cc_n = p.op("P", lambda e: e.collective_compute(
        "AllReduce", ALU.max, replica_groups=GROUPS,
        ins=[d_nin.opt()], outs=[d_nout.opt()]), deps=[ndma], sem="cc", inc=1)

    if DEBUG_DUMP:
        p.dma("Q", dbg_qk, d_qkgin, deps=part_write_toks[0] + part_write_toks[1])
        p.dma("Q", dbg_v, d_vgin, deps=part_write_toks[2])
        p.dma("Q", dbg_g, d_gscr, deps=g_write_toks)

    if STOP_AFTER < 2:
        p.replay()
        es_b1.close()
        es_ab.close()
        es.__exit__(None, None, None)
        return nc, dbg_names
    es_b1.close()
    es_ab.close()
    p.begin_block()
    es_b2 = ExitStack()
    sbuf2, psum2 = mk_alloc(es_b2)
    b_relm = sbuf2("b_relm", [128, RELW], F32)
    b_expB = [sbuf2("b_expB0", [128, RELW], BF16), sbuf2("b_expB1", [128, RELW], BF16)]
    b_qT = [sbuf2("b_qT0", [128, S], BF16), sbuf2("b_qT1", [128, S], BF16)]
    b_kT = [sbuf2("b_kT0", [128, S], BF16), sbuf2("b_kT1", [128, S], BF16)]
    b_vf = [sbuf2("b_vf0", [128, S], BF16), sbuf2("b_vf1", [128, S], BF16)]
    b_vt = sbuf2("b_vt", [128, S], BF16)
    b_P = [sbuf2("b_P0", [128, 1024], BF16), sbuf2("b_P1", [128, 1024], BF16)]
    b_rl = sbuf2("b_rl", [1, 512], F32)
    b_rlb = sbuf2("b_rlb", [128, 512], F32)
    oT_all = sbuf2("oT_all", [128, 8 * 4096], BF16)
    ps_sc = [psum2("ps_sc0", [128, 1024], F32), psum2("ps_sc1", [128, 1024], F32)]
    ps_l = psum2("ps_l", [1, 512], F32)
    ps_o = psum2("ps_o", [128, 512], F32)
    ps_rlb2 = psum2("ps_rlb2", [128, 512], F32)
    ps_vtb = psum2("ps_vtb", [128, 128], BF16)

    tk_relm = p.dma("Q", b_relm[:, :], d_relm)

    def mk_rank(e):
        p.shared["reg"] = [e.register(f"offreg{i}").__enter__() for i in range(3)]
        return e.memset(c_rank[0:1, 0:1], 0)
    rk_tok = p.op("P", mk_rank, deps=[tk_rk, tk_offs])

    def compact(dst, srcd, z, slot, mx):
        def fn(e):
            reg = p.shared["reg"][slot % 3]
            e.reg_load(reg, c_offs[0:1, slot:slot + 1])
            v = e.snap(reg, min_val=0, max_val=mx)
            inst = e.dma_start(
                out=dst.rearrange("(j z) t -> j z t", z=1024),
                in_=srcd.rearrange("(j z) t -> j z t", z=z)[:, ds(v, 1024), :])
            e.free_register(getattr(v, "val", v))
            return inst
        return fn
    cp_q = p.dma_dyn("P", compact(d_qmine, d_qkgout, 8192, 0, 3072), deps=[rk_tok, cc_qk])
    cp_k = p.dma_dyn("P", compact(d_kmine, d_qkgout, 8192, 1, 7168), deps=[])
    cp_v = p.dma_dyn("P", compact(d_vmine, d_vgout, 4096, 2, 3072), deps=[cc_v])
    cp_fence = p.op("P", lambda e: e.dma_start(out=s_row[0:1, 4:5], in_=s_row[0:1, 5:6]),
                    deps=[], sem="fdma", inc=16)

    nl = p.dma("Q", s_norm[0:1, :], d_nout, deps=[cc_n, ng])
    g1 = p.op("V", lambda e: e.tensor_scalar_max(s_r64a[0:1, :], s_norm[0:1, :], 1e-12), deps=[nl])
    lnn = p.op("S", lambda e: e.activation(s_r64a[0:1, :], s_r64a[0:1, :], AF.Ln), deps=[g1])
    sel_toks = []
    for hl in range(8):
        tq = p.op("V", lambda e, hl=hl: e.tensor_tensor(
            s_r64b[0:1, :], s_r64a[0:1, :], c_selq[0:1, hl * 64:(hl + 1) * 64], ALU.mult),
            deps=[lnn, tk_selq, sel_toks[-1] if sel_toks else None])
        tqs = p.op("V", lambda e, hl=hl: e.reduce_sum(s_q8[0:1, hl:hl + 1], s_r64b[0:1, :], axis=AX.X), deps=[tq])
        tk2 = p.op("V", lambda e, hl=hl: e.tensor_tensor(
            s_r64b[0:1, :], s_r64a[0:1, :], c_selk[0:1, hl * 64:(hl + 1) * 64], ALU.mult), deps=[tqs, tk_selk])
        tks = p.op("V", lambda e, hl=hl: e.reduce_sum(s_k8[0:1, hl:hl + 1], s_r64b[0:1, :], axis=AX.X), deps=[tk2])
        sel_toks.append(tks)
    cn1 = p.op("V", lambda e: e.tensor_tensor(s_q8[0:1, :], s_q8[0:1, :], s_k8[0:1, :], ALU.add), deps=sel_toks)
    cn2 = p.op("V", lambda e: e.tensor_scalar(s_q8[0:1, :], s_q8[0:1, :], 0.5, LN_SCALE, ALU.mult, ALU.add), deps=[cn1])
    cn3 = p.op("S", lambda e: e.activation(s_q8[0:1, :], s_q8[0:1, :], AF.Exp), deps=[cn2])
    cn4 = p.op("V", lambda e: e.tensor_scalar_mul(s_q8[0:1, :], s_q8[0:1, :], -1.0), deps=[cn3])
    cm = p.op("T", lambda e: e.matmul(ps_rlb2[:, 0:8], c_onesfr[:, :], s_q8[0:1, :], start=True, stop=True),
              deps=[cn4, tk_o2])
    negc_tok = p.op("S", lambda e: e.copy(s_negC[:, :], ps_rlb2[:, 0:8]), deps=[cm])

    load_users = [[], []]
    expB_users = [[], []]
    P_users = [[], []]
    psc_users = [[], []]
    psl_users = []
    pso_users = []
    psrlb_users = [negc_tok]
    vt_users = [None, None]
    vtb_users = []
    ow_toks = []

    for hl in range(8):
        bi = hl % 2
        ql = p.dma("Q", b_qT[bi][:, :].rearrange("p (j t) -> p j t", t=1024),
                   d_qmine.rearrange("(j h z) t -> j h z t", h=8, z=128)[:, hl, :, :].transpose([1, 0, 2]),
                   deps=[cp_fence] + load_users[bi])
        kl = p.dma("Q", b_kT[bi][:, :].rearrange("p (j t) -> p j t", t=1024),
                   d_kmine.rearrange("(j h z) t -> j h z t", h=8, z=128)[:, hl, :, :].transpose([1, 0, 2]),
                   deps=[])
        vl = p.dma("Q", b_vf[bi][:, :].rearrange("p (j t) -> p j t", t=1024),
                   d_vmine.rearrange("(j h z) t -> j h z t", h=8, z=128)[:, hl, :, :].transpose([1, 0, 2]),
                   deps=[])
        load_users[bi] = []

        eb = p.op("S", lambda e, hl=hl, bi=bi: e.activation(
            b_expB[bi][:, :], b_relm[:, :], AF.Exp, scale=c_slope[:, hl:hl + 1]),
            deps=[tk_relm, tk_slope] + expB_users[bi])
        expB_users[bi] = []

        vt_toks = []
        for dc in range(32):
            tr = p.op("T", lambda e, dc=dc, bi=bi: e.transpose(
                ps_vtb[:, :], b_vf[bi][:, dc * 128:(dc + 1) * 128], c_idb[:, :]),
                deps=[vl, tk_id] + vtb_users)
            vtb_users = []
            sel = dc % 2
            if sel == 0:
                ev = p.op("V", lambda e, dc=dc: e.tensor_copy(b_vt[:, dc * 128:(dc + 1) * 128], ps_vtb[:, :]),
                          deps=[tr, vt_users[0]])
            else:
                ev = p.op("S", lambda e, dc=dc: e.copy(b_vt[:, dc * 128:(dc + 1) * 128], ps_vtb[:, :]),
                          deps=[tr, vt_users[1]])
            vt_users[sel] = ev
            vtb_users = [ev]
            vt_toks.append(ev)

        head_onorms = []
        for qb in range(8):
            nch = 4 * (qb + 1)
            last_lm = None
            last_pv = None
            for cg in range(nch // 2):
                par = cg % 2
                scmms = []
                for half in range(2):
                    ci = cg * 2 + half
                    dep = ([ql, kl] + psc_users[par]) if half == 0 else []
                    scmms.append(p.op("T", lambda e, ci=ci, half=half, par=par, qb=qb, bi=bi: e.matmul(
                        ps_sc[par][:, half * 512:(half + 1) * 512],
                        b_kT[bi][:, ci * 128:(ci + 1) * 128],
                        b_qT[bi][:, qb * 512:(qb + 1) * 512],
                        start=True, stop=True), deps=dep))
                    if half == 0:
                        psc_users[par] = []
                ex = p.op("S", lambda e, par=par, hl=hl: e.activation(
                    b_P[par][:, :], ps_sc[par][:, :], AF.Exp, bias=s_negC[:, hl:hl + 1], scale=SCALE),
                    deps=scmms + [negc_tok] + P_users[par])
                P_users[par] = []
                psc_users[par].append(ex)
                tts = []
                for half in range(2):
                    ci = cg * 2 + half
                    c0 = 384 - 128 * ci + 512 * qb
                    tts.append(p.op("V", lambda e, par=par, half=half, c0=c0, bi=bi: e.tensor_tensor(
                        b_P[par][:, half * 512:(half + 1) * 512],
                        b_P[par][:, half * 512:(half + 1) * 512],
                        b_expB[bi][:, c0:c0 + 512], ALU.mult), deps=[ex, eb]))
                for half in range(2):
                    ci = cg * 2 + half
                    lm = p.op("T", lambda e, par=par, half=half, ci=ci, nch=nch: e.matmul(
                        ps_l[:, :], c_onesb[:, :], b_P[par][:, half * 512:(half + 1) * 512],
                        start=(ci == 0), stop=(ci == nch - 1)),
                        deps=[tts[half], tk_o1] + (psl_users if ci == 0 else []))
                    if ci == 0:
                        psl_users = []
                    pv = p.op("T", lambda e, par=par, half=half, ci=ci, nch=nch: e.matmul(
                        ps_o[:, :], b_vt[:, ci * 128:(ci + 1) * 128],
                        b_P[par][:, half * 512:(half + 1) * 512],
                        start=(ci == 0), stop=(ci == nch - 1)),
                        deps=[tts[half], vt_toks[ci]] + (pso_users if ci == 0 else []))
                    if ci == 0:
                        pso_users = []
                    P_users[par].extend([lm, pv])
                    last_lm, last_pv = lm, pv
            rc = p.op("V", lambda e: e.reciprocal(b_rl[0:1, :], ps_l[0:1, :]), deps=[last_lm])
            psl_users = psl_users + [rc]
            bm = p.op("T", lambda e: e.matmul(ps_rlb2[:, :], c_onesfr[:, :], b_rl[0:1, :],
                                              start=True, stop=True), deps=[rc, tk_o2] + psrlb_users)
            psrlb_users = []
            cb = p.op("S", lambda e: e.copy(b_rlb[:, :], ps_rlb2[:, :]), deps=[bm])
            psrlb_users.append(cb)
            onorm = p.op("V", lambda e, hl=hl, qb=qb: e.tensor_tensor(
                oT_all[:, hl * 4096 + qb * 512: hl * 4096 + qb * 512 + 512],
                ps_o[:, :], b_rlb[:, :], ALU.mult), deps=[cb, last_pv])
            pso_users.append(onorm)
            psrlb_users.append(onorm)
            head_onorms.append(onorm)
        ow = p.dma("Q", d_ogin[hl * 128:(hl + 1) * 128, :], oT_all[:, hl * 4096:(hl + 1) * 4096],
                   deps=head_onorms)
        ow_toks.append(ow)
        load_users[bi] = [head_onorms[-1]] + vt_toks

    cc_o = p.op("P", lambda e: e.collective_compute(
        "AllGather", ALU.bypass, replica_groups=GROUPS,
        ins=[d_ogin.opt()], outs=[d_ogout.opt()]), deps=ow_toks, sem="cc", inc=1)

    if DEBUG_DUMP:
        p.dma("Q", dbg_o, d_ogin, deps=ow_toks)
        p.dma("Q", dbg_n, d_nout, deps=[cc_n])

    if STOP_AFTER < 3:
        p.replay()
        es_b2.close()
        es.__exit__(None, None, None)
        return nc, dbg_names
    es_b2.close()
    p.begin_block()
    es_b3 = ExitStack()
    sbuf3, psum3 = mk_alloc(es_b3)
    cT = sbuf3("cT", [128, 32 * TPC], BF16)
    c_w = [sbuf3("c_w0", [128, 32 * 256], BF16), sbuf3("c_w1", [128, 32 * 256], BF16)]
    c_g = [sbuf3("c_g0", [128, 512], F32), sbuf3("c_g1", [128, 512], F32)]
    c_sg = sbuf3("c_sg", [128, 512], F32)
    c_ot = [sbuf3(f"c_ot{i}", [128, D], F32) for i in range(4)]
    c_xt = sbuf3("c_xt", [128, D], F32)
    ps_c = [psum3(f"ps_c{i}", [128, 512], F32) for i in range(4)]
    ps_ct = [psum3("ps_ct0", [128, 128], F32), psum3("ps_ct1", [128, 128], F32)]

    def dynload_c(q):
        def fn(e):
            reg = p.shared["reg"][0]
            e.reg_load(reg, c_offs[0:1, 30:31])
            v = e.snap(reg, min_val=0, max_val=3072)
            inst = e.dma_start(
                out=cT[:, :].rearrange("p (c t) -> p c t", t=1024)[:, q * 8:(q + 1) * 8, :],
                in_=d_ogout.rearrange("(c p) t -> p c t", p=128)[:, q * 8:(q + 1) * 8, ds(v, 1024)])
            e.free_register(getattr(v, "val", v))
            return inst
        return fn
    for q in range(4):
        otl = p.dma_dyn("P", dynload_c(q), deps=[cc_o, rk_tok] if q == 0 else [])

    cw_users = [[], []]
    psc3_users = [[], [], [], []]
    psct_users = [[], []]
    sg_users = []
    g_users = [[], []]
    ot_users = [[], [], [], []]
    xadd_users = []
    out_dma_toks = []

    for tg in range(2):
        for sgl in range(16):
            if tg * 16 + sgl >= C_ITERS:
                break
            wdeps = cw_users[sgl % 2]
            for q in range(4):
                wl = p.dma("P", c_w[sgl % 2][:, :].rearrange("p (c j) -> p c j", j=256)[:, q * 8:(q + 1) * 8, :],
                           d_wout[sgl].rearrange("c p j -> p c j")[:, q * 8:(q + 1) * 8, :], deps=wdeps)
                wdeps = []
            cw_users[sgl % 2] = []
            mm_last = {}
            for dc in range(32):
                for ou in range(2):
                    bk = (sgl % 2) * 2 + ou
                    dep = ([wl, otl] + psc3_users[bk]) if dc == 0 else [wl]
                    tok = p.op("T", lambda e, dc=dc, ou=ou, bk=bk, sgl=sgl, tg=tg: e.matmul(
                        ps_c[bk][:, :],
                        c_w[sgl % 2][:, dc * 256 + ou * 128: dc * 256 + ou * 128 + 128],
                        cT[:, dc * 1024 + tg * 512: dc * 1024 + tg * 512 + 512],
                        start=(dc == 0), stop=(dc == 31)), deps=dep)
                    if dc == 0:
                        psc3_users[bk] = []
                    cw_users[sgl % 2].append(tok)
                    mm_last[bk] = tok
            for ou in range(2):
                osub = sgl * 2 + ou
                bk = (sgl % 2) * 2 + ou
                gr = ou
                gl = p.dma("Q", c_g[gr][:, :],
                           d_gscr.rearrange("(c p) t -> p c t", p=128)[:, osub, tg * 512:(tg + 1) * 512],
                           deps=g_users[gr])
                sg_ = p.op("V", lambda e, bk=bk, gr=gr: e.tensor_tensor(
                    c_sg[:, :], ps_c[bk][:, :], c_g[gr][:, :], ALU.mult),
                    deps=[mm_last[bk], gl] + sg_users)
                sg_users = []
                g_users[gr] = [sg_]
                psc3_users[bk].append(sg_)
                for q4 in range(4 if C_TAIL else 0):
                    pi = q4 % 2
                    tr = p.op("T", lambda e, q4=q4, pi=pi: e.transpose(
                        ps_ct[pi][:, :], c_sg[:, q4 * 128:(q4 + 1) * 128], c_idf[:, :]),
                        deps=[sg_, tk_idf] + psct_users[pi])
                    psct_users[pi] = []
                    if q4 % 2 == 0:
                        ev = p.op("V", lambda e, q4=q4, pi=pi, osub=osub: e.tensor_copy(
                            c_ot[q4][:, osub * 128:(osub + 1) * 128], ps_ct[pi][:, :]),
                            deps=[tr] + ot_users[q4])
                    else:
                        ev = p.op("S", lambda e, q4=q4, pi=pi, osub=osub: e.copy(
                            c_ot[q4][:, osub * 128:(osub + 1) * 128], ps_ct[pi][:, :]),
                            deps=[tr] + ot_users[q4])
                    ot_users[q4] = [ev]
                    psct_users[pi].append(ev)
                    sg_users.append(ev)
        for q4 in range(4 if (C_TAIL and C_ITERS > 90) else 0):
            tt = tg * 4 + q4
            xl = p.dma("Q", c_xt[:, :], d_x[tt * 128:(tt + 1) * 128, :], deps=xadd_users)
            xa = p.op("V", lambda e, q4=q4: e.tensor_tensor(c_ot[q4][:, :], c_ot[q4][:, :],
                                                            c_xt[:, :], ALU.add),
                      deps=[xl] + ot_users[q4])
            od = p.dma("Q", d_out[tt * 128:(tt + 1) * 128, :], c_ot[q4][:, :], deps=[xa])
            ot_users[q4] = [od]
            xadd_users = [xa]
            out_dma_toks.append(od)

    if out_dma_toks:
        p.wait_only("Q", [out_dma_toks[-1]])
    p.replay()
    es_b3.close()
    es.__exit__(None, None, None)
    return nc, dbg_names


_BUILD_CACHE = {}


def _get_program():
    if "nc" not in _BUILD_CACHE:
        nc, dbg_names = build_program()
        _BUILD_CACHE["nc"] = nc
        _BUILD_CACHE["dbg"] = dbg_names
    return _BUILD_CACHE["nc"], _BUILD_CACHE["dbg"]


def prep_inputs(inputs):
    f32 = np.float32
    Wqkv = np.asarray(inputs["L0_attn_Wqkv"], f32)
    Wout = np.asarray(inputs["L0_attn_Wout"], f32)
    Wb = np.asarray(inputs["L0_ffn_W_base"], f32)
    Wg = np.asarray(inputs["L0_ffn_W_gate"], f32)
    Wt = np.asarray(inputs["L0_ffn_W_thr"], f32)
    Aw = np.asarray(inputs["L0_ffn_A_w"], f32)
    Bw = np.asarray(inputs["L0_ffn_B_w"], f32)
    X = np.asarray(inputs["L0_attn_hidden_states"], f32)

    qkv_g = np.ascontiguousarray(Wqkv.T.reshape(32, 128, 48, 256).transpose(2, 0, 1, 3))
    wbase_g = np.ascontiguousarray(Wb.T.reshape(32, 128, 16, 256).transpose(2, 0, 1, 3))
    wout_g = np.ascontiguousarray(Wout.T.reshape(32, 128, 16, 256).transpose(2, 0, 1, 3))
    wsm = np.ascontiguousarray(
        np.concatenate([Wg.T, Wt.reshape(D, 1), Aw.reshape(128, D).T], axis=1).reshape(32, 128, 137))
    bcat = np.ascontiguousarray(Bw.transpose(0, 2, 1).reshape(128, D))
    pc = lambda v: np.ascontiguousarray(np.asarray(v, f32).reshape(32, 128).T)
    b_base_pc = pc(inputs["L0_ffn_b_base"])
    qlw = pc(inputs["L0_attn_q_ln_w"])
    qlb = pc(inputs["L0_attn_q_ln_b"])
    klw = pc(inputs["L0_attn_k_ln_w"])
    klb = pc(inputs["L0_attn_k_ln_b"])
    bthr = np.ascontiguousarray(np.asarray(inputs["L0_ffn_b_thr"], f32).reshape(1, 1))

    pidx = np.arange(128)[:, None]
    cidx = np.arange(RELW)[None, :]
    rel = (pidx - cidx + 384).astype(f32)
    relm = np.ascontiguousarray(np.where(rel <= 0, rel, f32(MASKNEG)).astype(f32))
    identb = np.eye(128, dtype=f32).astype(ml_dtypes.bfloat16)
    identf = np.ascontiguousarray(np.eye(128, dtype=f32))

    in_maps = []
    for core in range(8):
        b, r = core // 4, core % 4
        slopes = np.tile(np.asarray([SLOPES[8 * r + hl] for hl in range(8)], f32), (128, 1))
        doffs = np.zeros((1, 32), np.int32)
        doffs[0, 0] = r * 1024
        doffs[0, 1] = 4096 + r * 1024
        doffs[0, 2] = r * 1024
        doffs[0, 30] = r * 1024
        selq = np.zeros((1, 512), f32)
        selk = np.zeros((1, 512), f32)
        for hl in range(8):
            selq[0, hl * 64 + 8 * r + hl] = 1.0
            selk[0, hl * 64 + 32 + 8 * r + hl] = 1.0
        in_maps.append({
            "x_rows": np.ascontiguousarray(X[b, r * TPC:(r + 1) * TPC, :]),
            "wqkv_g": qkv_g, "wbase_g": wbase_g, "wout_g": wout_g,
            "wsm_g": wsm, "bcat": bcat, "b_base_pc": b_base_pc,
            "q_ln_w_pc": qlw, "q_ln_b_pc": qlb, "k_ln_w_pc": klw, "k_ln_b_pc": klb,
            "b_thr_in": bthr, "relmaster": relm, "ident_bf": identb, "ident_f32": identf,
            "slopes_bc": np.ascontiguousarray(slopes), "selq": selq, "selk": selk,
            "rank_in": np.array([[r]], np.int32),
            "dyn_offs": doffs,
        })
    return in_maps


def kernel(**inputs):
    nc, dbg_names = _get_program()
    in_maps = prep_inputs(inputs)
    res = run_bass_kernel_spmd(nc, in_maps, core_ids=list(range(8)))
    out = np.zeros((B, S, D), np.float32)
    for core in range(8):
        b, r = core // 4, core % 4
        out[b, r * TPC:(r + 1) * TPC, :] = res.results[core]["out_rows"]
    if DEBUG_DUMP:
        kernel.debug = res.results
    return out
